# version 12
# speedup vs baseline: 1.6284x; 1.0307x over previous
import numpy as np
import ml_dtypes
from contextlib import ExitStack

import concourse.bass as bass
import concourse.mybir as mybir
from concourse.bass_utils import run_bass_kernel_spmd

F32 = mybir.dt.float32
BF16 = mybir.dt.bfloat16
FP8 = mybir.dt.float8e4
AF = mybir.ActivationFunctionType
ALU = mybir.AluOpType
AX = mybir.AxisListType

NTOK = 2048
D = 2048
H = 16
DH = 128
DFF = 8192
GA = 1024
GF = 512
NT_A = GA // 128
NT_F = GF // 128
EPS_LN = 1e-5
EPS_RMS = 1e-6
MXCLAMP = 4.48e-28


def build_nc(dbg=False):
    nc = bass.Bass("TRN2", target_bir_lowering=False)
    dbg_outs = {}
    def dbg_tensor(name, shape, dt):
        dbg_outs[name] = nc.dram_tensor("dbg_" + name, shape, dt, kind="ExternalOutput").ap()
        return dbg_outs[name]

    x_in = nc.dram_tensor("x", [NTOK, D], F32, kind="ExternalInput").ap()
    slope = nc.dram_tensor("slope", [H], F32, kind="ExternalInput").ap()
    wqkvT = nc.dram_tensor("wqkvT", [D, 3 * D], F32, kind="ExternalInput").ap()
    wgateT = nc.dram_tensor("wgateT", [D, D], F32, kind="ExternalInput").ap()
    woutT = nc.dram_tensor("woutT", [D, D], F32, kind="ExternalInput").ap()
    normw = nc.dram_tensor("normw", [D], F32, kind="ExternalInput").ap()
    gate_w = nc.dram_tensor("gate_w", [DFF, D], F32, kind="ExternalInput").ap()
    up_w = nc.dram_tensor("up_w", [DFF, D], F32, kind="ExternalInput").ap()
    down_w = nc.dram_tensor("down_w", [D, DFF], F32, kind="ExternalInput").ap()
    c_iota = nc.dram_tensor("c_iota", [128, 1], F32, kind="ExternalInput").ap()
    c_ident = nc.dram_tensor("c_ident", [128, 128], BF16, kind="ExternalInput").ap()
    c_tril = nc.dram_tensor("c_tril", [128, 128], BF16, kind="ExternalInput").ap()
    out_t = nc.dram_tensor("out", [NTOK, D], F32, kind="ExternalOutput").ap()
    cgateT = nc.dram_tensor("cgateT", [D, DFF], BF16, kind="Internal").ap()
    cupT = nc.dram_tensor("cupT", [D, DFF], BF16, kind="Internal").ap()
    cdownT = nc.dram_tensor("cdownT", [DFF, D], BF16, kind="Internal").ap()

    with ExitStack() as st:
        e = st.enter_context
        x_res = e(nc.sbuf_tensor("x_res", [128, NT_A * D], BF16))
        poolT = e(nc.sbuf_tensor("poolT", [128, 16 * GA], BF16))
        poolA = e(nc.sbuf_tensor("poolA", [128, NT_A * D], BF16))
        poolB = e(nc.sbuf_tensor("poolB", [128, NT_A * D], BF16))
        stage0 = e(nc.sbuf_tensor("stage0", [128, D], F32))
        stage1 = e(nc.sbuf_tensor("stage1", [128, D], F32))
        wsb = e(nc.sbuf_tensor("wsb", [128, 16, 512], BF16))
        trash = e(nc.sbuf_tensor("trash", [128, D], BF16))
        kvf = e(nc.sbuf_tensor("kvf", [128, H * DH], F32))
        kvb = e(nc.sbuf_tensor("kvb", [128, H * DH], BF16))
        qTt = e(nc.sbuf_tensor("qTt", [128, 2 * GA], BF16))
        kTt = e(nc.sbuf_tensor("kTt", [128, 2 * GA], BF16))
        h_f32 = e(nc.sbuf_tensor("h_f32", [128, 2, 512], F32))
        tmp_s = e(nc.sbuf_tensor("tmp_s", [128, 2, 512], BF16))
        tmp_u = e(nc.sbuf_tensor("tmp_u", [128, 2, 512], BF16))
        kdk = tmp_u.ap()[:, :, 0:256]
        qkm = trash.ap()[:, 0:768].rearrange("p (a c) -> p a c", c=384)
        tTd = e(nc.sbuf_tensor("tTd", [128, 2, 1024], BF16))
        hqv = e(nc.sbuf_tensor("hqv", [128, 2048], BF16))
        q8 = e(nc.sbuf_tensor("q8", [128, 1024], FP8))
        mxm = e(nc.sbuf_tensor("mxm", [128, 32], F32))
        mxr = e(nc.sbuf_tensor("mxr", [128, 32], F32))
        mxs = e(nc.sbuf_tensor("mxs", [128, 32], F32))
        sumx = e(nc.sbuf_tensor("sumx", [128, 16], F32))
        sumq = e(nc.sbuf_tensor("sumq", [128, 16], F32))
        mu = e(nc.sbuf_tensor("mu", [128, 16], F32))
        rstd = e(nc.sbuf_tensor("rstd", [128, 16], F32))
        nbias = e(nc.sbuf_tensor("nbias", [128, 16], F32))
        st1 = e(nc.sbuf_tensor("st1", [128, 16], F32))
        st2 = e(nc.sbuf_tensor("st2", [128, 16], F32))
        iota = e(nc.sbuf_tensor("iota", [128, 1], F32))
        ident = e(nc.sbuf_tensor("ident", [128, 128], BF16))
        tril = e(nc.sbuf_tensor("tril", [128, 128], BF16))
        sbc = e(nc.sbuf_tensor("sbc", [128, H], F32))
        nsbc = e(nc.sbuf_tensor("nsbc", [128, H], F32))
        ek = e(nc.sbuf_tensor("ek", [128, 2 * H], F32))
        eq = e(nc.sbuf_tensor("eq", [128, 2 * H], F32))
        kd = e(nc.sbuf_tensor("kd", [128, 2 * H], F32))
        bdt = e(nc.sbuf_tensor("bdt", [128, H], F32))
        nw_sb = e(nc.sbuf_tensor("nw_sb", [128, 16], F32))
        c256 = e(nc.sbuf_tensor("c256", [128, 1], F32))
        ps_d = e(nc.psum_tensor("ps_d", [128, 2048], F32))
        ps_a = e(nc.psum_tensor("ps_a", [128, 512], F32))
        ps_b = e(nc.psum_tensor("ps_b", [128, 512], F32))
        ps_ta = e(nc.psum_tensor("ps_ta", [128, 512], F32))
        ps_tb = e(nc.psum_tensor("ps_tb", [128, 512], F32))
        ps_qk = ps_d[:, 0:384]
        ps_o = ps_d[:, 512:768]
        ps_kv = ps_d[:, 1024:1152]

        sIN = e(nc.semaphore("sIN"))
        sGP = e(nc.semaphore("sGP"))
        sSD = e(nc.semaphore("sSD"))
        sPE = e(nc.semaphore("sPE"))
        sVE = e(nc.semaphore("sVE"))
        sSC = e(nc.semaphore("sSC"))
        sOUT = e(nc.semaphore("sOUT"))
        block = e(nc.Block())

        cnt = {s.name: 0 for s in (sIN, sGP, sSD, sPE, sVE, sSC, sOUT)}

        def inc(instr, sem, n=1):
            instr.then_inc(sem, n)
            cnt[sem.name] += n

        def val(sem):
            return cnt[sem.name]

        def wset(sem, v):
            cnt[sem.name] = v

        def dump(name, src_ap, shape, dt):
            if not dbg:
                return
            tgt = dbg_tensor(name, shape, dt)
            vv, vs, vp = val(sVE), val(sSC), val(sPE)
            vi, vd, vg = val(sIN), val(sSD), val(sGP)

            @block.sync
            def _(sy):
                sy.wait_ge(sVE, vv)
                sy.wait_ge(sSC, vs)
                sy.wait_ge(sPE, vp)
                sy.wait_ge(sIN, vi)
                sy.wait_ge(sSD, vd)
                sy.wait_ge(sGP, vg)
                inc(sy.dma_start(out=tgt, in_=src_ap), sOUT, 16)

        def vinc(ve):
            ve.drain()
            inc(ve.engine_nop(), sVE)

        def emit_mxdq(ve, src_f32, dst_bf, width, perm):
            nw = width // 32
            nb = perm // 32
            ve.tensor_reduce(mxm[:, 0:nw], src_f32.rearrange("p (w e) -> p w e", e=32),
                             AX.X, ALU.max, apply_absolute_value=True)
            ve.drain()
            ve.tensor_scalar_max(mxm[:, 0:nw], mxm[:, 0:nw], MXCLAMP)
            ve.drain()
            ve.reciprocal(mxr[:, 0:nw], mxm[:, 0:nw])
            ve.drain()
            ve.tensor_scalar_mul(mxr[:, 0:nw], mxr[:, 0:nw], 224.0)
            ve.tensor_scalar_mul(mxs[:, 0:nw], mxm[:, 0:nw], 1.0 / 224.0)
            ve.drain()
            last = None
            for c in range(width // perm):
                xv = src_f32[:, c * perm:(c + 1) * perm].rearrange(
                    "p (b e) -> p e b", e=32)
                qv = q8[:, c * perm:(c + 1) * perm].rearrange(
                    "p (e b) -> p e b", b=nb)
                dv = dst_bf[:, c * perm:(c + 1) * perm].rearrange(
                    "p (e b) -> p e b", b=nb)
                rv = mxr.ap()[:, c * nb:(c + 1) * nb].unsqueeze(1).broadcast_to(
                    [128, 32, nb])
                sv = mxs.ap()[:, c * nb:(c + 1) * nb].unsqueeze(1).broadcast_to(
                    [128, 32, nb])
                ve.tensor_tensor(qv, xv, rv, ALU.mult)
                last = ve.tensor_tensor(dv, qv, sv, ALU.mult)
            return last

        def norm_finish(n, eps, with_mean=True):
            @block.vector
            def _(ve):
                ve.drain()
                if with_mean:
                    ve.tensor_scalar_mul(mu[:, 0:n], sumx[:, 0:n], 1.0 / D)
                    ve.drain()
                    ve.tensor_tensor(st1[:, 0:n], mu[:, 0:n], mu[:, 0:n], ALU.mult)
                    ve.tensor_scalar_mul(st2[:, 0:n], sumq[:, 0:n], 1.0 / D)
                    ve.drain()
                    ve.tensor_tensor(st2[:, 0:n], st2[:, 0:n], st1[:, 0:n], ALU.subtract)
                else:
                    ve.tensor_scalar_mul(st2[:, 0:n], sumq[:, 0:n], 1.0 / D)
                ve.drain()
                inc(ve.tensor_scalar_add(st2[:, 0:n], st2[:, 0:n], eps), sVE)
            vq = val(sVE)

            @block.scalar
            def _(sc):
                sc.wait_ge(sVE, vq)
                inc(sc.activation(st1[:, 0:n], st2[:, 0:n], AF.Sqrt), sSC)
            vs = val(sSC)

            @block.vector
            def _(ve):
                ve.wait_ge(sSC, vs)
                ve.reciprocal(rstd[:, 0:n], st1[:, 0:n])
                ve.drain()
                ve.tensor_tensor(st1[:, 0:n], rstd[:, 0:n], rstd[:, 0:n], ALU.mult)
                ve.drain()
                ve.tensor_tensor(st1[:, 0:n], st1[:, 0:n], st2[:, 0:n], ALU.mult)
                ve.drain()
                ve.tensor_scalar(st1[:, 0:n], st1[:, 0:n], -0.5, 1.5, ALU.mult, ALU.add)
                ve.drain()
                last = ve.tensor_tensor(rstd[:, 0:n], rstd[:, 0:n], st1[:, 0:n], ALU.mult)
                if with_mean:
                    ve.drain()
                    ve.tensor_tensor(nbias[:, 0:n], mu[:, 0:n], rstd[:, 0:n], ALU.mult)
                    ve.drain()
                    last = ve.tensor_scalar_mul(nbias[:, 0:n], nbias[:, 0:n], -1.0)
                ve.drain()
                inc(last, sVE)

        def pe_transpose(src_fn, dst_fn, n, gp_wait=None):
            assert n % 4 == 0
            ng = n // 4
            pe0, ve0 = val(sPE), val(sVE)
            vsc0 = val(sSC)

            @block.tensor
            def _(te):
                te.wait_ge(sSC, vsc0)
                for g in range(ng):
                    bank = ps_ta if g % 2 == 0 else ps_tb
                    te.wait_ge(sVE, ve0 + max(0, g - 1))
                    for i in range(4):
                        m = te.matmul(bank[:, i * 128:(i + 1) * 128],
                                      src_fn(4 * g + i), ident[:, :],
                                      start=(i == 0), stop=(i == 3))
                    m.then_inc(sPE, 1)

            @block.vector
            def _(ve):
                if gp_wait is not None:
                    ve.wait_ge(sGP, gp_wait)
                for g in range(ng):
                    bank = ps_ta if g % 2 == 0 else ps_tb
                    ve.wait_ge(sPE, pe0 + g + 1)
                    ve.tensor_copy(dst_fn(g), bank.ap().rearrange(
                        "p (i j) -> p i j", j=128)).then_inc(sVE, 1)
            wset(sPE, pe0 + ng)
            wset(sVE, ve0 + ng)

        def proj_pipe(n_t, lhsT_fn, rhs_fn, nK, evac_fn, evac_eng="sc"):
            sEV = sSC if evac_eng == "sc" else sVE
            pe0, ev0 = val(sPE), val(sEV)
            vin_pe = val(sGP)
            vve = val(sVE)
            vsc = val(sSC)

            @block.tensor
            def _(te):
                te.wait_ge(sGP, vin_pe)
                te.wait_ge(sVE, vve)
                te.wait_ge(sSC, vsc)
                for t in range(n_t):
                    ps = ps_a if t % 2 == 0 else ps_b
                    te.wait_ge(sEV, ev0 + max(0, t - 1))
                    for fc in range(nK):
                        m = te.matmul(ps[:, :], lhsT_fn(t, fc), rhs_fn(fc),
                                      start=(fc == 0), stop=(fc == nK - 1))
                    m.then_inc(sPE, 1)

            body_eng = block.scalar if evac_eng == "sc" else block.vector

            @body_eng
            def _(eng):
                for t in range(n_t):
                    ps = ps_a if t % 2 == 0 else ps_b
                    eng.wait_ge(sPE, pe0 + t + 1)
                    inc(evac_fn(t, eng, ps), sEV)
            wset(sPE, pe0 + n_t)

        wv = wsb.ap()

        @block.sync
        def _(sy):
            inc(sy.dma_start(out=iota[:, :], in_=c_iota), sIN, 16)
            inc(sy.dma_start(out=ident[:, :], in_=c_ident), sIN, 16)
            inc(sy.dma_start(out=tril[:, :], in_=c_tril), sIN, 16)
            inc(sy.dma_start(out=nw_sb[:, :],
                             in_=normw.rearrange("(k p) -> p k", p=128),
                             allow_slow_non_contiguous=True), sIN, 16)

        @block.gpsimd
        def _(gp):
            inc(gp.dma_start(out=sbc[:, :],
                             in_=slope.unsqueeze(0).broadcast_to([128, H]),
                             allow_slow_non_contiguous=True), sGP, 16)
        v_sbc = val(sGP)

        @block.vector
        def _(ve):
            ve.wait_ge(sGP, v_sbc)
            ve.memset(kvf[:, :], 0.0)
            ve.memset(kvb[:, :], 0.0)
            ve.memset(c256[:, :], -256.0)
            ve.tensor_scalar_mul(nsbc[:, :], sbc[:, :], -1.0)
            ve.drain()
            for h in range(H):
                for i in range(2):
                    ve.tensor_scalar_mul(mxr[:, 2 * h + i: 2 * h + i + 1],
                                         sbc[:, h: h + 1], float(1 + 128 * i))
            ve.drain()
            vinc(ve)
        v1 = val(sVE)

        @block.scalar
        def _(sc):
            sc.wait_ge(sVE, v1)
            sc.wait_ge(sIN, 16)
            sc.activation(bdt[:, :], sbc[:, :], AF.Exp, scale=c256[:, :])
            for h in range(H):
                for i in range(2):
                    c = 2 * h + i
                    sc.activation(ek[:, c: c + 1], iota[:, :], AF.Exp,
                                  scale=sbc[:, h: h + 1], bias=mxr[:, c: c + 1])
                    _li = sc.activation(eq[:, c: c + 1], iota[:, :], AF.Exp,
                                  scale=nsbc[:, h: h + 1], bias=mxr[:, c: c + 1])
            inc(_li, sSC)
        v2 = val(sSC)

        @block.vector
        def _(ve):
            ve.wait_ge(sSC, v2)
            for h in range(H):
                for i in range(2):
                    ve.tensor_scalar_mul(mxr[:, 2 * h + i: 2 * h + i + 1],
                                         sbc[:, h: h + 1], float(1 + 128 * i - 256))
            ve.drain()
            vinc(ve)
        v3 = val(sVE)

        @block.scalar
        def _(sc):
            sc.wait_ge(sVE, v3)
            for h in range(H):
                for i in range(2):
                    c = 2 * h + i
                    _li = sc.activation(kd[:, c: c + 1], iota[:, :], AF.Exp,
                                  scale=sbc[:, h: h + 1], bias=mxr[:, c: c + 1])
            inc(_li, sSC)

        @block.vector
        def _(ve):
            wv3 = val(sSC)
            ve.wait_ge(sSC, wv3)
            for h in range(H):
                for i in range(2):
                    ve.tensor_scalar_mul(mxr[:, 2 * h + i: 2 * h + i + 1],
                                         nsbc[:, h: h + 1], float(1 + 128 * i))
            ve.drain()
            vinc(ve)
        v4 = val(sVE)

        @block.scalar
        def _(sc):
            sc.wait_ge(sVE, v4)
            for h in range(H):
                for i in range(2):
                    c = 2 * h + i
                    _li = sc.activation(eq[:, c: c + 1], iota[:, :], AF.Exp,
                                  scale=nsbc[:, h: h + 1], bias=mxr[:, c: c + 1])
            inc(_li, sSC)

        dump("sbc", sbc[:, :], [128, H], F32)
        dump("ek", ek[:, :], [128, 2 * H], F32)
        dump("eqt", eq[:, :], [128, 2 * H], F32)
        dump("kdt", kd[:, :], [128, 2 * H], F32)
        dump("bdt", bdt[:, :], [128, H], F32)

        def hqT_view():
            return hqv.ap()[:, 1024:1536].rearrange("p (c j) -> p c j", j=128)

        def prep_tT(i):
            return tTd.ap()[:, i % 2, :].rearrange("p (c j) -> p c j", j=128)

        prep_pieces = []
        for q in range(4):
            for r0 in range(q * 2048, (q + 1) * 2048, 128):
                for c0 in (0, 1024):
                    prep_pieces.append((cgateT, gate_w, r0, c0))
            for r0 in range(q * 2048, (q + 1) * 2048, 128):
                for c0 in (0, 1024):
                    prep_pieces.append((cupT, up_w, r0, c0))
            for r0 in range(0, 2048, 128):
                for c0 in (q * 2048, q * 2048 + 1024):
                    prep_pieces.append((cdownT, down_w, r0, c0))
        assert len(prep_pieces) == 384
        prep_state = {"idx": 0, "quad_done": [0, 0, 0, 0],
                      "dq_ve": [], "tr_pe": [], "st_gp": []}

        def emit_prep(k):
            for _ in range(k):
                i = prep_state["idx"]
                if i >= len(prep_pieces):
                    return
                cache_t, src_t, r0, c0 = prep_pieces[i]
                stg = stage0 if i % 2 == 0 else stage1
                half = (i % 2) * 1024
                dten, doff = (hqv, half) if i < 344 else (poolB, 8192 + half)
                w_stage = max(prep_state["dq_ve"][i - 2] if i >= 2 else val(sVE),
                              prep_state.get("stage_floor", 0))

                @block.sync
                def _(sy, src_t=src_t, r0=r0, c0=c0, stg=stg, w_stage=w_stage):
                    sy.wait_ge(sVE, w_stage)
                    inc(sy.dma_start(out=stg[:, 0:1024],
                                     in_=src_t[r0:r0 + 128, c0:c0 + 1024]), sIN, 16)
                vin = val(sIN)
                w_hqv = (prep_state["tr_pe"][i - 2] if i >= 2 else 0)

                perm = 512 if cache_t is cdownT else 1024

                @block.vector
                def _(ve, stg=stg, vin=vin, dten=dten, doff=doff, w_hqv=w_hqv,
                      perm=perm):
                    ve.wait_ge(sIN, vin)
                    ve.wait_ge(sPE, w_hqv)
                    inc(emit_mxdq(ve, stg[:, 0:1024],
                                  dten[:, doff: doff + 1024], 1024, perm), sVE)
                prep_state["dq_ve"].append(val(sVE))
                pe_transpose(
                    lambda ii, dten=dten, doff=doff:
                        dten[:, doff + ii * 128: doff + (ii + 1) * 128],
                    lambda g2, i=i: prep_tT(i)[:, 4 * g2: 4 * g2 + 4, :], 8,
                    gp_wait=(prep_state["st_gp"][i - 2] if i >= 2 else 0))
                prep_state["tr_pe"].append(val(sPE))
                vdq = val(sVE)

                @block.gpsimd
                def _(gp, cache_t=cache_t, r0=r0, c0=c0, i=i, vdq=vdq):
                    gp.wait_ge(sVE, vdq)
                    inc(gp.dma_start(
                        out=cache_t[c0:c0 + 1024, r0:r0 + 128]
                        .rearrange("(fi p) j -> p fi j", p=128),
                        in_=prep_tT(i)[:, :, :]), sGP, 16)
                prep_state["st_gp"].append(val(sGP))
                prep_state["idx"] = i + 1
                if (i + 1) % 96 == 0:
                    prep_state["quad_done"][(i + 1) // 96 - 1] = val(sGP)

        def prep_quad_wait(qi):
            return prep_state["quad_done"][qi]

        ffn_roll = {"a3_pe": 0, "d3_pe": 0, "silu_done": 0, "vcopy_done": 0,
                    "acc_done": 0}
        kv_done = {h: 0 for h in range(H)}

        for g in range(2):
            tok0 = g * GA

            for t in range(NT_A):
                stg = stage0 if t % 2 == 0 else stage1
                vv_prev = val(sVE)

                @block.sync
                def _(sy, t=t, stg=stg, tok0=tok0, vv_prev=vv_prev):
                    sy.wait_ge(sVE, vv_prev)
                    inc(sy.dma_start(out=stg[:, :],
                                     in_=x_in[tok0 + t * 128: tok0 + (t + 1) * 128, :]),
                        sIN, 16)
                vin = val(sIN)

                @block.vector
                def _(ve, t=t, stg=stg, vin=vin):
                    ve.wait_ge(sIN, vin)
                    ve.reduce_sum(sumx[:, t: t + 1], stg[:, :], AX.X)
                    ve.scalar_tensor_tensor(trash[:, :], stg[:, :], 1.0, stg[:, :],
                                            ALU.mult, ALU.mult,
                                            accum_out=sumq[:, t: t + 1])
                    inc(ve.tensor_copy(x_res[:, t * D:(t + 1) * D], stg[:, :]), sVE)

            norm_finish(NT_A, EPS_LN)
            v_ln1 = val(sVE)

            for t in range(NT_A):
                @block.vector
                def _(ve, t=t):
                    inc(ve.tensor_scalar(trash[:, :], x_res[:, t * D:(t + 1) * D],
                                         rstd[:, t: t + 1], nbias[:, t: t + 1],
                                         ALU.mult, ALU.add), sVE)
                pe_transpose(
                    lambda i: trash[:, i * 128:(i + 1) * 128],
                    lambda g, t=t: poolT.ap().rearrange("p (i q) -> p i q", q=GA)[:, 4 * g: 4 * g + 4, t * 128:(t + 1) * 128],
                    16)

            if g == 0:
                dump("xres_a", x_res[:, :], [128, NT_A * D], BF16)
                dump("rstd_ln1", rstd[:, :], [128, 16], F32)
                dump("ln1T", poolT[:, :], [128, 16 * GA], BF16)

            for n in range(4):
                vpe_prev = val(sPE)

                @block.gpsimd
                def _(gp, n=n, vpe_prev=vpe_prev):
                    gp.wait_ge(sPE, vpe_prev)
                    inc(gp.dma_start(
                        out=wv,
                        in_=wqkvT[:, 2 * D + n * 512: 2 * D + (n + 1) * 512]
                        .rearrange("(c p) n -> p c n", p=128)), sGP, 16)
                proj_pipe(
                    NT_A,
                    lambda t, fc: poolT[:, fc * GA + t * 128: fc * GA + (t + 1) * 128],
                    lambda fc: wv[:, fc, :],
                    16,
                    lambda t, sc, ps, n=n: sc.activation(
                        poolA[:, t * D + n * 512: t * D + (n + 1) * 512],
                        ps[:, :], AF.Silu))
                emit_prep(10)

            if g == 0:
                dump("v_all", poolA[:, :], [128, NT_A * D], BF16)

            last_proj_pe = None
            for w4 in range(8):
                emit_prep(28)
                heads = [2 * w4 + i for i in range(2)]
                for pair in range(1):
                    hp = heads
                    vpe_prev = last_proj_pe if last_proj_pe is not None else val(sPE)

                    @block.gpsimd
                    def _(gp, hp=hp, vpe_prev=vpe_prev):
                        gp.wait_ge(sPE, vpe_prev)
                        for ii, h in enumerate(hp):
                            inc(gp.dma_start(
                                out=wv[:, :, ii * 256: ii * 256 + 128],
                                in_=wqkvT[:, h * 128:(h + 1) * 128]
                                .rearrange("(c p) n -> p c n", p=128)), sGP, 16)
                            inc(gp.dma_start(
                                out=wv[:, :, ii * 256 + 128: ii * 256 + 256],
                                in_=wqkvT[:, D + h * 128: D + (h + 1) * 128]
                                .rearrange("(c p) n -> p c n", p=128)), sGP, 16)
                    vw = val(sGP)
                    pe0, sc0 = val(sPE), val(sSC)

                    @block.tensor
                    def _(te, pair=pair, vw=vw, pe0=pe0, sc0=sc0):
                        te.wait_ge(sGP, vw)
                        step = 0
                        for ii in range(2):
                            for which in range(2):
                                for n in range(2):
                                    ps = ps_a if step % 2 == 0 else ps_b
                                    te.wait_ge(sSC, sc0 + max(0, step - 1))
                                    for fc in range(16):
                                        m = te.matmul(
                                            ps[:, :],
                                            wv[:, fc, ii * 256 + which * 128:
                                               ii * 256 + (which + 1) * 128],
                                            poolT[:, fc * GA + n * 512:
                                                  fc * GA + (n + 1) * 512],
                                            start=(fc == 0), stop=(fc == 15))
                                    m.then_inc(sPE, 1)
                                    step += 1

                    @block.scalar
                    def _(sc, pe0=pe0):
                        step = 0
                        for ii in range(2):
                            for which, dst in ((0, qTt), (1, kTt)):
                                for n in range(2):
                                    ps = ps_a if step % 2 == 0 else ps_b
                                    sc.wait_ge(sPE, pe0 + step + 1)
                                    inc(sc.activation(
                                        dst[:, ii * GA + n * 512:
                                            ii * GA + (n + 1) * 512],
                                        ps[:, :], AF.Silu), sSC)
                                    step += 1
                    wset(sPE, pe0 + 8)
                    last_proj_pe = pe0 + 8

                for b4 in range(4):
                    t0, t1 = 2 * b4, 2 * b4 + 1
                    vq = b4 * 256
                    stc = {}

                    def emit_qk(wi, b4=b4, vq=vq, stc=stc):
                        h = heads[wi]
                        vsc_b = val(sSC)
                        vve0 = val(sVE)

                        @block.tensor
                        def _(te, wi=wi, vq=vq, vsc_b=vsc_b, vve0=vve0):
                            te.wait_ge(sSC, vsc_b)
                            te.wait_ge(sVE, vve0)
                            m = te.matmul(ps_qk[:, 0:256],
                                          kTt[:, wi * GA + vq: wi * GA + vq + 128],
                                          qTt[:, wi * GA + vq: wi * GA + vq + 256],
                                          start=True, stop=False)
                            m = te.matmul(ps_qk[:, 256:384],
                                          kTt[:, wi * GA + vq + 128: wi * GA + vq + 256],
                                          qTt[:, wi * GA + vq + 128: wi * GA + vq + 256],
                                          start=False, stop=True)
                            m.then_inc(sPE, 1)
                        wset(sPE, val(sPE) + 1)
                        vp = val(sPE)

                        @block.vector
                        def _(ve, wi=wi, h=h, vp=vp):
                            ve.wait_ge(sPE, vp)
                            ve.scalar_tensor_tensor(qkm[:, wi % 2, 0:128], ps_qk[:, 0:128],
                                                    ek[:, 2 * h: 2 * h + 1], tril[:, :],
                                                    ALU.mult, ALU.mult)
                            ve.tensor_scalar_mul(qkm[:, wi % 2, 128:256], ps_qk[:, 128:256],
                                                 ek[:, 2 * h: 2 * h + 1])
                            inc(ve.scalar_tensor_tensor(
                                qkm[:, wi % 2, 256:384], ps_qk[:, 256:384],
                                ek[:, 2 * h + 1: 2 * h + 2], tril[:, :],
                                ALU.mult, ALU.mult), sVE)
                        stc[("qkm", wi)] = val(sVE)

                    def emit_kdk(wi, b4=b4, vq=vq, stc=stc):
                        h = heads[wi]
                        vve_t = val(sVE)

                        @block.tensor
                        def _(te, wi=wi, vq=vq, vve_t=vve_t):
                            te.wait_ge(sVE, vve_t)
                            m = te.matmul(ps_ta[:, 0:128],
                                          kTt[:, wi * GA + vq: wi * GA + vq + 128],
                                          ident[:, :], start=True, stop=False)
                            m = te.matmul(ps_ta[:, 128:256],
                                          kTt[:, wi * GA + vq + 128: wi * GA + vq + 256],
                                          ident[:, :], start=False, stop=True)
                            m.then_inc(sPE, 1)
                        wset(sPE, val(sPE) + 1)
                        vp = val(sPE)

                        @block.vector
                        def _(ve, wi=wi, h=h, vp=vp):
                            ve.wait_ge(sPE, vp)
                            ve.tensor_scalar_mul(kdk[:, wi % 2, 0:128], ps_ta[:, 0:128],
                                                 kd[:, 2 * h: 2 * h + 1])
                            inc(ve.tensor_scalar_mul(kdk[:, wi % 2, 128:256],
                                                     ps_ta[:, 128:256],
                                                     kd[:, 2 * h + 1: 2 * h + 2]), sVE)
                        stc[("kdk", wi)] = val(sVE)

                    def emit_oev(wi, b4=b4, vq=vq, t0=t0, t1=t1, stc=stc):
                        h = heads[wi]

                        @block.tensor
                        def _(te, wi=wi, h=h, vq=vq, t0=t0, t1=t1,
                              vkm=stc[("qkm", wi)], vkv=kv_done[h]):
                            te.wait_ge(sVE, max(vkm, kv_done[h]))
                            te.matmul(ps_o[:, 0:128],
                                      qTt[:, wi * GA + vq: wi * GA + vq + 128],
                                      kvb[:, h * DH:(h + 1) * DH],
                                      start=True, stop=False)
                            te.matmul(ps_o[:, 128:256],
                                      qTt[:, wi * GA + vq + 128: wi * GA + vq + 256],
                                      kvb[:, h * DH:(h + 1) * DH],
                                      start=False, stop=False)
                            te.matmul(ps_o[:, 0:128], qkm[:, wi % 2, 0:128],
                                      poolA[:, t0 * D + h * 128: t0 * D + (h + 1) * 128],
                                      start=False, stop=False)
                            te.matmul(ps_o[:, 128:256], qkm[:, wi % 2, 128:256],
                                      poolA[:, t0 * D + h * 128: t0 * D + (h + 1) * 128],
                                      start=False, stop=False)
                            m = te.matmul(ps_o[:, 128:256], qkm[:, wi % 2, 256:384],
                                          poolA[:, t1 * D + h * 128: t1 * D + (h + 1) * 128],
                                          start=False, stop=True)
                            m.then_inc(sPE, 1)
                        wset(sPE, val(sPE) + 1)
                        vp2 = val(sPE)

                        @block.vector
                        def _(ve, wi=wi, h=h, vp2=vp2, t0=t0, t1=t1):
                            ve.wait_ge(sPE, vp2)
                            ve.tensor_scalar_mul(
                                poolB[:, t0 * D + h * 128: t0 * D + (h + 1) * 128],
                                ps_o[:, 0:128], eq[:, 2 * h: 2 * h + 1])
                            inc(ve.tensor_scalar_mul(
                                poolB[:, t1 * D + h * 128: t1 * D + (h + 1) * 128],
                                ps_o[:, 128:256], eq[:, 2 * h + 1: 2 * h + 2]), sVE)

                    def emit_kv(wi, b4=b4, t0=t0, t1=t1, stc=stc):
                        h = heads[wi]

                        @block.tensor
                        def _(te, wi=wi, h=h, t0=t0, t1=t1, vkdk=stc[("kdk", wi)]):
                            te.wait_ge(sVE, vkdk)
                            te.matmul(ps_kv[:, :], kdk[:, wi % 2, 0:128],
                                      poolA[:, t0 * D + h * 128: t0 * D + (h + 1) * 128],
                                      start=True, stop=False)
                            m = te.matmul(ps_kv[:, :], kdk[:, wi % 2, 128:256],
                                          poolA[:, t1 * D + h * 128: t1 * D + (h + 1) * 128],
                                          start=False, stop=True)
                            m.then_inc(sPE, 1)
                        wset(sPE, val(sPE) + 1)
                        vp4 = val(sPE)

                        @block.vector
                        def _(ve, h=h, vp4=vp4):
                            ve.wait_ge(sPE, vp4)
                            ve.scalar_tensor_tensor(kvf[:, h * DH:(h + 1) * DH],
                                                    kvf[:, h * DH:(h + 1) * DH],
                                                    bdt[:, h: h + 1], ps_kv[:, :],
                                                    ALU.mult, ALU.add)
                            ve.drain()
                            inc(ve.tensor_copy(kvb[:, h * DH:(h + 1) * DH],
                                               kvf[:, h * DH:(h + 1) * DH]), sVE)
                        kv_done[h] = val(sVE)

                    emit_qk(0)
                    emit_qk(1)
                    emit_oev(0)
                    emit_kdk(0)
                    emit_kv(0)
                    emit_oev(1)
                    emit_kdk(1)
                    emit_kv(1)

            for n in range(4):
                vpe_prev = val(sPE)

                @block.gpsimd
                def _(gp, n=n, vpe_prev=vpe_prev):
                    gp.wait_ge(sPE, vpe_prev)
                    inc(gp.dma_start(
                        out=wv,
                        in_=wgateT[:, n * 512:(n + 1) * 512]
                        .rearrange("(c p) n -> p c n", p=128)), sGP, 16)
                proj_pipe(
                    NT_A,
                    lambda t, fc: poolT[:, fc * GA + t * 128: fc * GA + (t + 1) * 128],
                    lambda fc: wv[:, fc, :],
                    16,
                    lambda t, sc, ps, n=n: sc.activation(
                        poolA[:, t * D + n * 512: t * D + (n + 1) * 512],
                        ps[:, :], AF.Sigmoid))
                emit_prep(10)

            if g == 0:
                dump("gate_sb", poolA[:, :], [128, NT_A * D], BF16)

            v_gate_done = val(sSC)

            @block.vector
            def _(ve, v_gate_done=v_gate_done):
                ve.wait_ge(sSC, v_gate_done)
                last = None
                for t in range(NT_A):
                    last = ve.scalar_tensor_tensor(trash[:, :],
                                                   poolB[:, t * D:(t + 1) * D],
                                                   1.0, poolB[:, t * D:(t + 1) * D],
                                                   ALU.mult, ALU.mult,
                                                   accum_out=sumq[:, t: t + 1])
                inc(last, sVE)
            norm_finish(NT_A, EPS_RMS, with_mean=False)
            v_rms = val(sVE)

            for t in range(NT_A):
                emit_prep(5)

                @block.vector
                def _(ve, t=t, v_rms=v_rms):
                    ve.wait_ge(sVE, v_rms)
                    ve.tensor_tensor(poolA[:, t * D:(t + 1) * D],
                                     poolA[:, t * D:(t + 1) * D],
                                     poolB[:, t * D:(t + 1) * D], ALU.mult)
                    ve.drain()
                    inc(ve.tensor_scalar_mul(poolA[:, t * D:(t + 1) * D],
                                             poolA[:, t * D:(t + 1) * D],
                                             rstd[:, t: t + 1]), sVE)
                pe_transpose(
                    lambda i, t=t: poolA[:, t * D + i * 128: t * D + (i + 1) * 128],
                    lambda g, t=t: poolT.ap().rearrange("p (i q) -> p i q", q=GA)[:, 4 * g: 4 * g + 4, t * 128:(t + 1) * 128],
                    16)

            for n in range(4):
                vpe_prev = val(sPE)

                @block.gpsimd
                def _(gp, n=n, vpe_prev=vpe_prev):
                    gp.wait_ge(sPE, vpe_prev)
                    inc(gp.dma_start(
                        out=wv,
                        in_=woutT[:, n * 512:(n + 1) * 512]
                        .rearrange("(c p) n -> p c n", p=128)), sGP, 16)
                vw0 = val(sGP)

                @block.vector
                def _(ve, vw0=vw0):
                    ve.wait_ge(sGP, vw0)
                    last = None
                    for fc in range(16):
                        last = ve.tensor_scalar_mul(wv[:, fc, :], wv[:, fc, :],
                                                    nw_sb[:, fc: fc + 1])
                    inc(last, sVE)
                vwf = val(sVE)
                pe0, ve0 = val(sPE), val(sVE)
                vsc0 = val(sSC)

                @block.tensor
                def _(te, vwf=vwf, pe0=pe0, ve0=ve0, vsc0=vsc0):
                    te.wait_ge(sVE, vwf)
                    te.wait_ge(sSC, vsc0)
                    for t in range(NT_A):
                        ps = ps_a if t % 2 == 0 else ps_b
                        te.wait_ge(sVE, ve0 + max(0, t - 1))
                        for fc in range(16):
                            m = te.matmul(
                                ps[:, :],
                                poolT[:, fc * GA + t * 128: fc * GA + (t + 1) * 128],
                                wv[:, fc, :], start=(fc == 0), stop=(fc == 15))
                        m.then_inc(sPE, 1)

                @block.vector
                def _(ve, n=n, pe0=pe0):
                    for t in range(NT_A):
                        ps = ps_a if t % 2 == 0 else ps_b
                        ve.wait_ge(sPE, pe0 + t + 1)
                        inc(ve.tensor_copy(
                            poolB[:, t * D + n * 512: t * D + (n + 1) * 512],
                            ps[:, :]), sVE)
                wset(sPE, pe0 + NT_A)

            @block.vector
            def _(ve):
                last = None
                for t in range(NT_A):
                    ve.reduce_sum(sumx[:, t: t + 1], poolB[:, t * D:(t + 1) * D], AX.X)
                    last = ve.scalar_tensor_tensor(trash[:, :],
                                                   poolB[:, t * D:(t + 1) * D],
                                                   1.0, poolB[:, t * D:(t + 1) * D],
                                                   ALU.mult, ALU.mult,
                                                   accum_out=sumq[:, t: t + 1])
                inc(last, sVE)
            norm_finish(NT_A, EPS_LN)
            v_lny = val(sVE)

            @block.vector
            def _(ve):
                for t in range(NT_A):
                    ve.tensor_scalar(trash[:, :], poolB[:, t * D:(t + 1) * D],
                                     rstd[:, t: t + 1], nbias[:, t: t + 1],
                                     ALU.mult, ALU.add)
                    ve.drain()
                    inc(ve.tensor_tensor(x_res[:, t * D:(t + 1) * D],
                                         x_res[:, t * D:(t + 1) * D], trash[:, :],
                                         ALU.add), sVE)

            if g == 0:
                dump("y_sb", poolB[:, :], [128, NT_A * D], BF16)
                dump("x2", x_res[:, :], [128, NT_A * D], BF16)

            if g == 0:
                assert prep_state["idx"] == 344

            for sub in range(2):
                ts0 = sub * NT_F
                acc_off = 16 * GF

                v_prev = val(sVE)

                @block.vector
                def _(ve, ts0=ts0, v_prev=v_prev):
                    ve.wait_ge(sVE, v_prev)
                    for j in range(NT_F):
                        t = ts0 + j
                        ve.reduce_sum(sumx[:, j: j + 1], x_res[:, t * D:(t + 1) * D],
                                      AX.X)
                        last = ve.scalar_tensor_tensor(trash[:, :],
                                                       x_res[:, t * D:(t + 1) * D],
                                                       1.0, x_res[:, t * D:(t + 1) * D],
                                                       ALU.mult, ALU.mult,
                                                       accum_out=sumq[:, j: j + 1])
                    inc(last, sVE)
                norm_finish(NT_F, EPS_LN)
                v_ln2 = val(sVE)

                for j in range(NT_F):
                    t = ts0 + j
                    for half in range(2):
                        @block.vector
                        def _(ve, t=t, j=j, half=half):
                            ve.tensor_scalar(
                                stage0[:, half * 1024:(half + 1) * 1024],
                                x_res[:, t * D + half * 1024: t * D + (half + 1) * 1024],
                                rstd[:, j: j + 1], nbias[:, j: j + 1],
                                ALU.mult, ALU.add)
                            inc(emit_mxdq(ve,
                                          stage0[:, half * 1024:(half + 1) * 1024],
                                          trash[:, half * 1024:(half + 1) * 1024],
                                          1024, 1024), sVE)
                    pe_transpose(
                        lambda i: trash[:, i * 128:(i + 1) * 128],
                        lambda g, j=j: poolT.ap().rearrange("p (i q) -> p i q", q=GF)[:, 4 * g: 4 * g + 4, j * 128:(j + 1) * 128],
                        16)

                if g == 0 and sub == 0:
                    dump("ln2qT", poolT[:, 0:16 * GF], [128, 16 * GF], BF16)

                roll = ffn_roll
                prep_state["stage_floor"] = val(sVE)
                for sc_i in range(16):
                    if g == 0 and sub == 0 and sc_i < 8:
                        emit_prep(10)
                    o0 = sc_i * 512
                    vquad = prep_quad_wait(sc_i // 4)
                    vsc_w, vve_w = val(sSC), val(sVE)

                    @block.sync
                    def _(sy, o0=o0, sc_i=sc_i, vsc_w=vsc_w, vve_w=vve_w,
                          vquad=vquad, a3=roll["a3_pe"]):
                        sy.wait_ge(sPE, a3)
                        sy.wait_ge(sGP, vquad)
                        if sc_i == 0:
                            sy.wait_ge(sSC, vsc_w)
                            sy.wait_ge(sVE, vve_w)
                        inc(sy.dma_start(
                            out=poolA[:, 0:8192].rearrange("p (c n) -> p c n", n=512),
                            in_=cgateT[:, o0: o0 + 512]
                            .rearrange("(c p) n -> p c n", p=128)), sIN, 16)
                        inc(sy.dma_start(
                            out=poolA[:, 8192:16384].rearrange("p (c n) -> p c n", n=512),
                            in_=cupT[:, o0: o0 + 512]
                            .rearrange("(c p) n -> p c n", p=128)), sIN, 16)

                    @block.sync
                    def _(sy, o0=o0, d3=roll["d3_pe"], sc_i=sc_i, vve_w=vve_w):
                        sy.wait_ge(sPE, d3)
                        if sc_i == 0:
                            sy.wait_ge(sVE, vve_w)
                        inc(sy.dma_start(
                            out=poolB[:, 0:8192].rearrange("p (c n) -> p c n", n=2048),
                            in_=cdownT[o0: o0 + 512, :]
                            .rearrange("(c p) n -> p c n", p=128)), sIN, 16)
                    vw = val(sIN)

                    st = {}

                    def emit_A(j, sc_i=sc_i, vw=vw, st=st):
                        w_silu = roll["silu_done"]
                        w_vcopy = roll["vcopy_done"]

                        @block.tensor
                        def _(te, j=j, vw=vw, w_silu=w_silu, w_vcopy=w_vcopy):
                            te.wait_ge(sIN, vw)
                            te.wait_ge(sSC, w_silu)
                            te.wait_ge(sVE, w_vcopy)
                            for fc in range(16):
                                m = te.matmul(
                                    ps_a[:, :],
                                    poolT[:, fc * GF + j * 128: fc * GF + (j + 1) * 128],
                                    poolA[:, fc * 512:(fc + 1) * 512],
                                    start=(fc == 0), stop=(fc == 15))
                            m.then_inc(sPE, 1)
                            for fc in range(16):
                                m = te.matmul(
                                    ps_b[:, :],
                                    poolT[:, fc * GF + j * 128: fc * GF + (j + 1) * 128],
                                    poolA[:, 8192 + fc * 512: 8192 + (fc + 1) * 512],
                                    start=(fc == 0), stop=(fc == 15))
                            m.then_inc(sPE, 1)
                        st[("A", j)] = val(sPE) + 2
                        wset(sPE, val(sPE) + 2)
                        if j == 3:
                            roll["a3_pe"] = val(sPE)

                    def emit_silu(j, st=st):
                        slot = j % 2
                        w_vmx = (st[("vmx", j - 2)] if j >= 2 else
                                 (roll.get("vmx_roll2", 0) if j == 0 else
                                  roll.get("vmx_roll1", 0)))

                        @block.scalar
                        def _(sc, j=j, slot=slot, vpA=st[("A", j)], w_vmx=w_vmx):
                            sc.wait_ge(sPE, vpA)
                            sc.wait_ge(sVE, w_vmx)
                            inc(sc.activation(tmp_s[:, slot, :], ps_a[:, :], AF.Silu),
                                sSC)
                        st[("silu", j)] = val(sSC)
                        roll["silu_done"] = val(sSC)

                    def emit_vcopy(j, st=st):
                        slot = j % 2

                        @block.vector
                        def _(ve, slot=slot, vpA=st[("A", j)]):
                            ve.wait_ge(sPE, vpA)
                            inc(ve.tensor_copy(tmp_u[:, slot, :], ps_b[:, :]), sVE)
                        st[("vcopy", j)] = val(sVE)
                        roll["vcopy_done"] = val(sVE)

                    def emit_vmx(j, st=st):
                        slot = j % 2
                        hq_guard = (st[("T", j - 2)] if j >= 2 else
                                    (roll.get("t_roll2", 0) if j == 0 else
                                     roll.get("t_roll1", 0)))

                        @block.vector
                        def _(ve, j=j, slot=slot, vsj=st[("silu", j)],
                              hq_guard=hq_guard):
                            ve.wait_ge(sSC, vsj)
                            ve.wait_ge(sPE, hq_guard)
                            ve.tensor_tensor(h_f32[:, slot, :], tmp_s[:, slot, :],
                                             tmp_u[:, slot, :], ALU.mult)
                            inc(emit_mxdq(ve, h_f32[:, slot, :],
                                          hqv[:, slot * 512:(slot + 1) * 512],
                                          512, 512), sVE)
                        st[("vmx", j)] = val(sVE)
                        roll["vmx_roll2"] = roll.get("vmx_roll1", 0)
                        roll["vmx_roll1"] = val(sVE)

                    def emit_T(j, st=st):
                        slot = j % 2
                        pe_transpose(
                            lambda i, slot=slot: hqv[:, slot * 512 + i * 128:
                                                     slot * 512 + (i + 1) * 128],
                            lambda gg: hqT_view()[:, :, :], 4)
                        st[("T", j)] = val(sPE)
                        roll["t_roll2"] = roll.get("t_roll1", 0)
                        roll["t_roll1"] = val(sPE)

                    def emit_D(j, sc_i=sc_i, st=st):
                        w_acc = roll["acc_done"]

                        @block.tensor
                        def _(te, w_acc=w_acc):
                            te.wait_ge(sVE, val(sVE))
                            for n in range(4):
                                for k in range(4):
                                    m = te.matmul(
                                        ps_d[:, n * 512:(n + 1) * 512], hqT_view()[:, k, :],
                                        poolB[:, k * 2048 + n * 512: k * 2048 + (n + 1) * 512],
                                        start=(k == 0), stop=(k == 3))
                            m.then_inc(sPE, 1)
                        st[("D", j)] = val(sPE) + 1
                        wset(sPE, val(sPE) + 1)
                        if j == 3:
                            roll["d3_pe"] = val(sPE)

                    def emit_acc(j, sc_i=sc_i, st=st):
                        @block.vector
                        def _(ve, sc_i=sc_i, j=j, vpd=st[("D", j)]):
                            ve.wait_ge(sPE, vpd)
                            dst = poolT[:, acc_off + j * D: acc_off + (j + 1) * D]
                            if sc_i == 0:
                                inc(ve.tensor_copy(dst, ps_d[:, :]), sVE)
                            else:
                                inc(ve.tensor_tensor(dst, dst, ps_d[:, :], ALU.add),
                                    sVE)
                        roll["acc_done"] = val(sVE)

                    def stepjj():
                        roll["jj"] += 1
                        roll["vmx2_done"] = roll["vmx_prev"]
                        roll["vmx_prev"] = val(sVE)

                    emit_A(0); emit_silu(0); emit_vcopy(0)
                    emit_A(1); emit_silu(1); emit_vcopy(1)
                    emit_vmx(0); emit_T(0); emit_D(0); emit_acc(0)
                    emit_A(2); emit_silu(2); emit_vcopy(2)
                    emit_vmx(1); emit_T(1); emit_D(1); emit_acc(1)
                    emit_A(3); emit_silu(3); emit_vcopy(3)
                    emit_vmx(2); emit_T(2); emit_D(2); emit_acc(2)
                    emit_vmx(3); emit_T(3); emit_D(3); emit_acc(3)

                if g == 0 and sub == 0:
                    dump("acc", poolT[:, 16 * GF: 16 * GF + NT_F * D],
                         [128, NT_F * D], BF16)

                v_acc = val(sVE)

                @block.vector
                def _(ve, v_acc=v_acc, acc_off=acc_off):
                    ve.wait_ge(sVE, v_acc)
                    for j in range(NT_F):
                        src = poolT[:, acc_off + j * D: acc_off + (j + 1) * D]
                        ve.reduce_sum(sumx[:, j: j + 1], src, AX.X)
                        last = ve.scalar_tensor_tensor(trash[:, :], src, 1.0, src,
                                                       ALU.mult, ALU.mult,
                                                       accum_out=sumq[:, j: j + 1])
                    inc(last, sVE)
                norm_finish(NT_F, EPS_LN)
                v_lnf = val(sVE)

                for j in range(NT_F):
                    t = ts0 + j
                    vout_prev = val(sOUT)

                    @block.vector
                    def _(ve, t=t, j=j, acc_off=acc_off, vout_prev=vout_prev):
                        ve.tensor_scalar(trash[:, :],
                                         poolT[:, acc_off + j * D: acc_off + (j + 1) * D],
                                         rstd[:, j: j + 1], nbias[:, j: j + 1],
                                         ALU.mult, ALU.add)
                        ve.drain()
                        ve.wait_ge(sOUT, vout_prev)
                        inc(ve.tensor_tensor(stage1[:, :], x_res[:, t * D:(t + 1) * D],
                                             trash[:, :], ALU.add), sVE)
                    vv = val(sVE)

                    @block.sync
                    def _(sy, t=t, vv=vv, tok0=tok0):
                        sy.wait_ge(sVE, vv)
                        inc(sy.dma_start(
                            out=out_t[tok0 + t * 128: tok0 + (t + 1) * 128, :],
                            in_=stage1[:, :]), sOUT, 16)

        dump("cgate_s", cgateT[0:128, :], [128, DFF], BF16)
        dump("cdown_s", cdownT[0:128, 0:1024], [128, 1024], BF16)

        @block.sync
        def _(sy):
            sy.wait_ge(sOUT, val(sOUT))

    return nc


_NC_CACHE = None


def kernel(L0_attn_hidden_states, L0_attn_slope_rate, L0_attn_w_qkv,
           L0_attn_w_gate, L0_attn_w_out, L0_attn_norm_w,
           L0_ffn_gate_w, L0_ffn_up_w, L0_ffn_down_w):
    global _NC_CACHE
    x = np.ascontiguousarray(np.asarray(L0_attn_hidden_states, dtype=np.float32))
    slope = np.ascontiguousarray(np.asarray(L0_attn_slope_rate, dtype=np.float32))
    wqkvT = np.ascontiguousarray(np.asarray(L0_attn_w_qkv, dtype=np.float32).T)
    wgateT = np.ascontiguousarray(np.asarray(L0_attn_w_gate, dtype=np.float32).T)
    woutT = np.ascontiguousarray(np.asarray(L0_attn_w_out, dtype=np.float32).T)
    normw = np.ascontiguousarray(np.asarray(L0_attn_norm_w, dtype=np.float32))
    gate_w = np.ascontiguousarray(np.asarray(L0_ffn_gate_w, dtype=np.float32))
    up_w = np.ascontiguousarray(np.asarray(L0_ffn_up_w, dtype=np.float32))
    down_w = np.ascontiguousarray(np.asarray(L0_ffn_down_w, dtype=np.float32))

    iota = np.arange(128, dtype=np.float32).reshape(128, 1)
    ident = np.eye(128, dtype=ml_dtypes.bfloat16)
    tr = np.tril(np.ones((128, 128), dtype=np.float32)).T.astype(ml_dtypes.bfloat16)

    if _NC_CACHE is None:
        _NC_CACHE = build_nc()
    nc = _NC_CACHE

    base = {"slope": slope, "wqkvT": wqkvT, "wgateT": wgateT, "woutT": woutT,
            "normw": normw, "gate_w": gate_w, "up_w": up_w, "down_w": down_w,
            "c_iota": iota, "c_ident": ident, "c_tril": tr}
    in_maps = [dict(base, x=np.ascontiguousarray(x[c])) for c in range(8)]
    res = run_bass_kernel_spmd(nc, in_maps, core_ids=list(range(8)))
    out = np.stack([np.asarray(res.results[c]["out"]) for c in range(8)], axis=0)
    return out.astype(np.float32)
